# version 21
# baseline (speedup 1.0000x reference)
import sys
sys.path.insert(0, '/opt/trn_rl_repo')
import numpy as np
import concourse.bass as bass
import concourse.mybir as mybir
from concourse.bass_utils import run_bass_kernel_spmd
from contextlib import ExitStack

F32 = mybir.dt.float32
BF16 = mybir.dt.bfloat16
AF = mybir.ActivationFunctionType
ALU = mybir.AluOpType
AX = mybir.AxisListType

NCORES = 8
D = 1024
NE = 8
NH = 16
EPS_LN = 1e-5
EPS_RMS = 1.1920929e-07
SCALE_ATT = 0.125


class Buf:
    __slots__ = ("w", "rs", "name")

    def __init__(self, name):
        self.name = name
        self.w = None
        self.rs = []


def build_program(seq=2048, bpc=2, debug=False):
    NT = seq // 128
    NK = seq // 2
    NKB = NK // 128
    HALF = min(1024, seq)
    NHALF = seq // HALF
    NCH = HALF // 512
    NTG = HALF // 128

    nc = bass.Bass("TRN2", target_bir_lowering=False, debug=False)

    d_x = nc.dram_tensor("x", [bpc * seq, D], F32, kind="ExternalInput").ap()
    d_qwT = nc.dram_tensor("qwT", [D, D], F32, kind="ExternalInput").ap()
    d_kwT = nc.dram_tensor("kwT", [D, D], F32, kind="ExternalInput").ap()
    d_vwT = nc.dram_tensor("vwT", [D, D], F32, kind="ExternalInput").ap()
    d_owT = nc.dram_tensor("owT", [D, D], F32, kind="ExternalInput").ap()
    d_w1 = nc.dram_tensor("w1", [NE, D, 2048], F32, kind="ExternalInput").ap()
    d_w2 = nc.dram_tensor("w2", [NE, 2048, D], F32, kind="ExternalInput").ap()
    d_gwT = nc.dram_tensor("gwT", [D, NE], F32, kind="ExternalInput").ap()
    d_b1T = nc.dram_tensor("b1T", [128, 16 * NE], F32, kind="ExternalInput").ap()
    d_b2 = nc.dram_tensor("b2", [NE, D], F32, kind="ExternalInput").ap()
    d_qb = nc.dram_tensor("qb", [128, 8], F32, kind="ExternalInput").ap()
    d_kb = nc.dram_tensor("kb", [128, 8], F32, kind="ExternalInput").ap()
    d_vb = nc.dram_tensor("vb", [1, D], F32, kind="ExternalInput").ap()
    d_ob = nc.dram_tensor("ob", [1, D], F32, kind="ExternalInput").ap()
    d_gb = nc.dram_tensor("gb", [1, NE], F32, kind="ExternalInput").ap()
    d_rw = nc.dram_tensor("rw", [1, NE], F32, kind="ExternalInput").ap()
    d_cw = nc.dram_tensor("cw", [128, 24], F32, kind="ExternalInput").ap()
    d_cb = nc.dram_tensor("cb", [128, 8], F32, kind="ExternalInput").ap()
    d_ident = nc.dram_tensor("ident", [128, 128], F32, kind="ExternalInput").ap()
    d_w1b = nc.dram_tensor("w1b", [NE, D, 2048], BF16, kind="Internal").ap()
    d_w2b = nc.dram_tensor("w2b", [NE, 2048, D], BF16, kind="Internal").ap()
    d_awb = nc.dram_tensor("awb", [4, D, D], BF16, kind="Internal").ap()
    d_b2b = nc.dram_tensor("b2b", [NE, D], BF16, kind="Internal").ap()
    d_out = nc.dram_tensor("out", [bpc * seq, D], F32, kind="ExternalOutput").ap()
    dbg = {}
    if debug:
        dbg["xn"] = nc.dram_tensor("dbg_xn", [128, 8 * seq], F32, kind="ExternalOutput").ap()
        dbg["ng"] = nc.dram_tensor("dbg_ng", [128, NT], F32, kind="ExternalOutput").ap()
        dbg["kf"] = nc.dram_tensor("dbg_kf", [128, 8 * NK], F32, kind="ExternalOutput").ap()
        dbg["vpx"] = nc.dram_tensor("dbg_vpx", [128, NKB * 1040], F32, kind="ExternalOutput").ap()
        dbg["y"] = nc.dram_tensor("dbg_y", [128, NTG * D], F32, kind="ExternalOutput").ap()

    with ExitStack() as es:
        sb = lambda name, shape, dt: es.enter_context(nc.sbuf_tensor(name, shape, dt))
        ps = lambda name, shape, dt: es.enter_context(nc.psum_tensor(name, shape, dt))

        xn = sb("xn", [128, 8, seq], BF16)
        yb = sb("yb", [128, NTG, D], F32)
        kf = sb("kf", [128, 8, NK], BF16)
        vpx = sb("vpx", [128, NKB, 1040], BF16)
        W1H = sb("W1H", [128, 8, 1024], BF16)
        W2H = sb("W2H", [128, 8, 1024], BF16)
        regA = sb("regA", [128, 8, 1024], BF16)
        xt_sl = [sb(f"xt{i}", [128, D], F32) for i in range(2)]
        xn_tm = [sb(f"xntm{i}", [128, D], BF16) for i in range(2)]
        qt_sb = sb("qt", [128, 8, 512], BF16)
        Xt = [sb(f"Xt{i}", [128, 1024], BF16) for i in range(2)]
        attn_o = sb("attn_o", [128, 8, 512], BF16)
        RB = [sb(f"RB{i}", [128, 512], F32) for i in range(2)]
        sg_sl = [sb(f"sg{i}", [128, 512], F32) for i in range(2)]
        recips = sb("recips", [1, 2 * 512], BF16)
        rtmp = sb("rtmp", [1, 512], F32)
        junk = sb("junk", [128, D], BF16)
        onesF = sb("onesF", [1, 128], F32)
        onesB = sb("onesB", [1, 128], BF16)
        ones8 = sb("ones8", [8, 1], BF16)
        selA = sb("selA", [1, 128], BF16)
        selB = sb("selB", [1, 128], BF16)
        identS = sb("identS", [128, 128], BF16)
        gw = sb("gws", [128, 8, NE], BF16)
        b1T = sb("b1Ts", [128, 16, NE], F32)
        qb = sb("qbs", [128, 8], F32)
        kb = sb("kbs", [128, 8], F32)
        cw = sb("cws", [128, 8, 3], F32)
        cb = sb("cbs", [128, 8], F32)
        gbr = sb("gbr", [1, NE], F32)
        rwr = sb("rwr", [1, NE], F32)
        rowA = sb("rowA", [1, D], BF16)
        vbrow = sb("vbrow", [1, D], BF16)
        obrow = sb("obrow", [1, D], BF16)
        GB = sb("GB", [128, NE], F32)
        RW2 = sb("RW2", [128, NE], F32)
        ng = sb("ng", [128, NT], F32)
        lnst = sb("lnst", [128, 2, 8], F32)
        gst = sb("gst", [128, 2, 8], F32)
        gsc = sb("gsc", [128, 2, NE], F32)
        gz2 = sb("gz2", [128, 2, NE], F32)
        junk8 = sb("junk8", [128, NE], F32)
        epsc = sb("epsc", [128, 4], F32)
        cacc = [sb(f"cacc{i}", [128, NK], F32) for i in range(2)]

        pSC0 = ps("pSC0", [128, 1024], F32)
        pSC1 = ps("pSC1", [128, 1024], F32)
        pC = ps("pC", [128, 512], F32)
        pD = ps("pD", [128, 512], F32)
        pE = ps("pE", [128, 512], F32)
        pF = ps("pF", [128, 512], F32)
        pSC1b = pSC1.bitcast(BF16)
        trslot = [pSC1b[:, 0:128], pSC1b[:, 1024:1152]]

        LS = [es.enter_context(nc.semaphore(f"LS{i}")) for i in range(2)]
        LG = [es.enter_context(nc.semaphore(f"LG{i}")) for i in range(2)]
        PE = es.enter_context(nc.semaphore("PE"))
        SC = es.enter_context(nc.semaphore("SC"))
        VE = es.enter_context(nc.semaphore("VE"))
        block = es.enter_context(nc.Block())

        cnt = {}
        for s in LS + LG + [PE, SC, VE]:
            cnt[id(s)] = 0
        dma_sems = {'sync': LS, 'gpsimd': LG}
        dma_idx = {'sync': 0, 'gpsimd': 0}
        eng_sem = {'tensor': PE, 'scalar': SC, 'vector': VE}
        items = []

        def emit(eng, fn, reads=(), writes=(), extra=()):
            waits = []
            for b in reads:
                if b.w is not None:
                    waits.append(b.w)
            for b in writes:
                if b.w is not None:
                    waits.append(b.w)
                waits.extend(b.rs)
            waits.extend(e for e in extra if e is not None)
            if eng in dma_sems:
                sem = dma_sems[eng][dma_idx[eng] % 2]
                dma_idx[eng] += 1
                if cnt[id(sem)] > 0:
                    waits.append((sem, cnt[id(sem)]))
                cnt[id(sem)] += 16
            else:
                sem = eng_sem[eng]
                cnt[id(sem)] += 1
            ev = (sem, cnt[id(sem)])
            items.append((eng, waits, fn, sem, ev[1]))
            for b in reads:
                b.rs.append(ev)
            for b in writes:
                b.w = ev
                b.rs = []
            return ev

        bXN = Buf("xn"); bY = Buf("y"); bKF = Buf("kf"); bVPX = Buf("vpx")
        bW1 = Buf("W1H"); bW2 = Buf("W2H"); bA = Buf("regA")
        bXT = [Buf("xt0"), Buf("xt1")]; bTM = [Buf("tm0"), Buf("tm1")]
        bQT = [Buf(f"qt{i}") for i in range(8)]
        bXt = [Buf("Xt0"), Buf("Xt1")]
        bAO = Buf("attn_o"); bRB = [Buf("RB0"), Buf("RB1")]
        bSG = [Buf("sg0"), Buf("sg1")]; bREC = Buf("recips")
        bJ = Buf("junk"); bJ8 = Buf("junk8"); bRT = Buf("rtmp")
        bONE = Buf("onesF"); bONEB = Buf("onesB"); bONE8 = Buf("ones8"); bSelA = Buf("selA"); bSelB = Buf("selB"); bID = Buf("ident")
        bGW = Buf("gw"); bB1T = Buf("b1T"); bQB = Buf("qb"); bKB = Buf("kb")
        bCW = Buf("cw"); bCB = Buf("cb"); bGBr = Buf("gbr"); bRWr = Buf("rwr")
        bRowA = Buf("rowA"); bVB = Buf("vbrow"); bOB = Buf("obrow")
        bGB = Buf("GB"); bRW2 = Buf("RW2")
        bNG = Buf("ng"); bLN = Buf("lnst"); bGS = Buf("gst")
        bGSC = [Buf("gsc0"), Buf("gsc1")]; bGZ = [Buf("gz0"), Buf("gz1")]
        bEPS = Buf("epsc"); bCA = [Buf("cacc0"), Buf("cacc1")]
        bS0a = Buf("pSC0a"); bS0b = Buf("pSC0b")
        bS1a = Buf("pSC1a"); bS1b = Buf("pSC1b")
        bpC = Buf("pC"); bpD = Buf("pD"); bpE = Buf("pE"); bpF = Buf("pF")
        bOUT = Buf("d_out")

        ev_xcopy = None
        emit('gpsimd', lambda e: e.dma_start(out=gw[:, :, :],
                                             in_=d_gwT.rearrange("(c p) e -> p c e", p=128)),
             writes=[bGW])
        emit('gpsimd', lambda e: e.dma_start(out=identS[:, :], in_=d_ident[:, :]),
             writes=[bID])
        for dst, src, b in [
            (b1T[:, :, :], d_b1T.rearrange("p (c e) -> p c e", e=NE), bB1T),
            (qb[:, :], d_qb[:, :], bQB), (kb[:, :], d_kb[:, :], bKB),
            (cw[:, :, :], d_cw.rearrange("p (c t) -> p c t", t=3), bCW),
            (cb[:, :], d_cb[:, :], bCB),
            (gbr[:, :], d_gb[:, :], bGBr), (rwr[:, :], d_rw[:, :], bRWr),
        ]:
            emit('sync', (lambda ds, sr: lambda e: e.dma_start(out=ds, in_=sr))(dst, src),
                 writes=[b])
        emit('vector', lambda e: e.memset(onesF[:, :], 1.0), writes=[bONE])
        emit('vector', lambda e: e.memset(onesB[:, :], 1.0), writes=[bONEB])
        emit('vector', lambda e: e.memset(ones8[:, :], 1.0), writes=[bONE8])
        emit('vector', lambda e: e.memset(epsc[:, 0:1], EPS_LN), writes=[bEPS])
        emit('vector', lambda e: e.memset(epsc[:, 1:2], EPS_RMS), reads=[], writes=[bEPS])
        emit('vector', lambda e: e.memset(selA[:, :], 0.0), writes=[bSelA])
        emit('vector', lambda e: e.memset(selA[0:1, 0:64], 1.0), writes=[bSelA])
        emit('vector', lambda e: e.memset(selB[:, :], 0.0), writes=[bSelB])
        emit('vector', lambda e: e.memset(selB[0:1, 64:128], 1.0), writes=[bSelB])
        for kc in range(NKB):
            emit('vector',
                 (lambda kc: lambda e: e.memset(
                     vpx[:, kc, :].rearrange("p (h c) -> p h c", c=65)[:, :, 64:65], 1.0))(kc),
                 writes=[bVPX])
        emit('tensor', lambda e: e.matmul(pC[:, 0:NE], onesF[0:1, :], gbr[0:1, :],
                                          start=True, stop=True),
             reads=[bONE, bGBr], writes=[bpC])
        emit('vector', lambda e: e.tensor_copy(out=GB[:, :], in_=pC[:, 0:NE]),
             reads=[bpC], writes=[bGB])
        emit('tensor', lambda e: e.matmul(pC[:, 0:NE], onesF[0:1, :], rwr[0:1, :],
                                          start=True, stop=True),
             reads=[bONE, bRWr], writes=[bpC])
        emit('vector', lambda e: e.tensor_scalar(out=RW2[:, :], in0=pC[:, 0:NE],
                                                 scalar1=2.0, scalar2=None, op0=ALU.mult),
             reads=[bpC], writes=[bRW2])

        emit('gpsimd', lambda e: e.dma_start(out=vbrow[:, :], in_=d_vb[:, :]),
             writes=[bVB])
        emit('gpsimd', lambda e: e.dma_start(out=obrow[:, :], in_=d_ob[:, :]),
             writes=[bOB])

        stg = [(attn_o, [bAO]), (qt_sb, bQT)]
        ev_aw = [None] * 4
        ev_w1p = [[[], []] for _ in range(NE)]
        ev_w2p = [[[], []] for _ in range(NE)]
        pp_i = 0

        def prepass(src_ap, dst_ap, wide=False):
            nonlocal pp_i
            st, stb = stg[pp_i % 2]
            pp_i += 1
            stv = st.rearrange("p (x y) c -> p x (y c)", y=2) if wide else st
            emit('gpsimd', (lambda stv, src_ap: lambda e: e.dma_start(
                    out=stv[:, :, :], in_=src_ap))(stv, src_ap), writes=stb)
            return emit('gpsimd', (lambda stv, dst_ap: lambda e: e.dma_start(
                    out=dst_ap, in_=stv[:, :, :]))(stv, dst_ap), reads=stb)

        qfl = qt_sb.rearrange("p a c -> p (a c)")
        emit('gpsimd', lambda e: e.dma_start(out=qfl[0:NE, 0:D], in_=d_b2[:, :]),
             writes=bQT)
        for hf in range(2):
            emit('tensor', (lambda hf: lambda e: e.matmul(
                    pC[:1, 0:512], ones8[:, :],
                    qfl[0:NE, hf * 512:(hf + 1) * 512],
                    start=True, stop=True))(hf),
                 reads=bQT + [bONE8], writes=[bpC])
            emit('scalar', (lambda hf: lambda e: e.activation(
                    out=rowA[0:1, hf * 512:(hf + 1) * 512], in_=pC[:1, 0:512],
                    func=AF.Identity))(hf),
                 reads=[bpC], writes=[bRowA])

        for i, dsrc in enumerate([d_qwT, d_kwT, d_vwT, d_owT]):
            evs = []
            for q in range(2):
                evs.append(prepass(
                    dsrc[:, q * 512:(q + 1) * 512].rearrange("(c p) o -> p c o", p=128),
                    d_awb[i, :, q * 512:(q + 1) * 512].rearrange("(c p) o -> p c o", p=128)))
            ev_aw[i] = evs
        for e_i in range(NE):
            for uq in range(4):
                ev = prepass(
                    d_w1[e_i, :, uq * 512:(uq + 1) * 512]
                    .rearrange("(c p) h -> p c h", p=128),
                    d_w1b[e_i, :, uq * 512:(uq + 1) * 512]
                    .rearrange("(c p) h -> p c h", p=128))
                ev_w1p[e_i][uq // 2].append(ev)
            for uq in range(4):
                ev = prepass(
                    d_w2[e_i, uq * 512:(uq + 1) * 512, :]
                    .rearrange("(a p) d -> p a d", p=128),
                    d_w2b[e_i, uq * 512:(uq + 1) * 512, :]
                    .rearrange("(a p) d -> p a d", p=128), wide=True)
                ev_w2p[e_i][uq // 2].append(ev)

        for b in range(bpc):
            brow = b * seq

            def gs(i, j):
                return gst[:, i % 2, j:j + 1]

            def emit_gate(i):
                s = i % 2
                t0 = i * 128
                emit('tensor', (lambda t0: lambda e: [
                        e.matmul(pC[:, 0:NE], xn[:, cc, t0:t0 + 128], gw[:, cc, :],
                                 start=(cc == 0), stop=(cc == 7))
                        for cc in range(8)][-1])(t0),
                     reads=[bXN, bGW], writes=[bpC])
                emit('vector', (lambda s: lambda e: e.tensor_tensor(
                        out=gsc[:, s, :], in0=pC[:, 0:NE], in1=GB[:, :], op=ALU.add))(s),
                     reads=[bpC, bGB], writes=[bGSC[s]])
                emit('scalar', (lambda s, i: lambda e: e.activation(
                        out=junk8[:, :], in_=gsc[:, s, :], func=AF.Square,
                        accum_out=gs(i, 0)))(s, i),
                     reads=[bGSC[s]], writes=[bJ8, bGS])
                emit('scalar', (lambda i: lambda e: e.activation(
                        out=gs(i, 1), in_=gs(i, 0), func=AF.Ln, scale=1.0 / NE,
                        bias=epsc[:, 1:2]))(i),
                     reads=[bGS, bEPS], writes=[bGS])
                emit('scalar', (lambda i: lambda e: e.activation(
                        out=gs(i, 2), in_=gs(i, 1), func=AF.Exp, scale=-0.5))(i),
                     reads=[bGS], writes=[bGS])
                emit('vector', (lambda s, i: lambda e: e.scalar_tensor_tensor(
                        out=gz2[:, s, :], in0=gsc[:, s, :], scalar=gs(i, 2),
                        in1=RW2[:, :], op0=ALU.mult, op1=ALU.mult))(s, i),
                     reads=[bGSC[s], bGS, bRW2], writes=[bGZ[s]])
                emit('vector', (lambda s, i: lambda e: e.reduce_max(
                        out=gs(i, 3), in_=gz2[:, s, :], axis=AX.X, negate=True))(s, i),
                     reads=[bGZ[s]], writes=[bGS])
                emit('scalar', (lambda s, i: lambda e: e.activation(
                        out=junk8[:, :], in_=gz2[:, s, :], func=AF.Exp, bias=gs(i, 3),
                        accum_out=gs(i, 4)))(s, i),
                     reads=[bGZ[s], bGS], writes=[bJ8, bGS])
                emit('vector', (lambda i: lambda e: e.tensor_scalar(
                        out=gs(i, 5), in0=gs(i, 4), scalar1=1e-6, scalar2=1.0,
                        op0=ALU.mult, op1=ALU.add))(i),
                     reads=[bGS], writes=[bGS])
                emit('vector', (lambda i: lambda e: e.reciprocal(
                        out=ng[:, i:i + 1], in_=gs(i, 5)))(i),
                     reads=[bGS], writes=[bNG])


            def sl(i, j):
                return lnst[:, i % 2, j:j + 1]
            for i in range(NT):
                s = i % 2
                t0 = brow + i * 128
                evld = emit('sync',
                            (lambda s, t0: lambda e: e.dma_start(
                                out=xt_sl[s][:, :], in_=d_x[t0:t0 + 128, :]))(s, t0),
                            writes=[bXT[s]])
                emit('vector', (lambda s, i: lambda e: e.reduce_sum(
                        out=sl(i, 0), in_=xt_sl[s][:, :], axis=AX.X))(s, i),
                     reads=[bXT[s]], writes=[bLN])
                emit('scalar', (lambda s, i: lambda e: e.activation(
                        out=junk[:, :], in_=xt_sl[s][:, :], func=AF.Square,
                        accum_out=sl(i, 1)))(s, i),
                     reads=[bXT[s]], writes=[bJ, bLN])
                emit('vector', (lambda i: lambda e: e.tensor_scalar(
                        out=sl(i, 2), in0=sl(i, 0), scalar1=1.0 / D, scalar2=None,
                        op0=ALU.mult))(i),
                     reads=[bLN], writes=[bLN])
                emit('scalar', (lambda i: lambda e: e.activation(
                        out=sl(i, 3), in_=sl(i, 2), func=AF.Square))(i),
                     reads=[bLN], writes=[bLN])
                emit('vector', (lambda i: lambda e: e.scalar_tensor_tensor(
                        out=sl(i, 4), in0=sl(i, 1), scalar=1.0 / D, in1=sl(i, 3),
                        op0=ALU.mult, op1=ALU.subtract))(i),
                     reads=[bLN], writes=[bLN])
                emit('scalar', (lambda i: lambda e: e.activation(
                        out=sl(i, 5), in_=sl(i, 4), func=AF.Ln, bias=epsc[:, 0:1]))(i),
                     reads=[bLN, bEPS], writes=[bLN])
                emit('scalar', (lambda i: lambda e: e.activation(
                        out=sl(i, 6), in_=sl(i, 5), func=AF.Exp, scale=-0.5))(i),
                     reads=[bLN], writes=[bLN])
                emit('vector', (lambda s, i: lambda e: e.tensor_scalar(
                        out=xn_tm[s][:, :], in0=xt_sl[s][:, :], scalar1=sl(i, 2),
                        scalar2=sl(i, 6), op0=ALU.subtract, op1=ALU.mult))(s, i),
                     reads=[bXT[s], bLN], writes=[bTM[s]])
                for cc in range(8):
                    p = cc % 2
                    pslot = trslot[p]
                    pbuf = [bS1a, bS1b][p]
                    emit('tensor', (lambda s, cc, pslot: lambda e: e.transpose(
                            pslot, xn_tm[s][:, cc * 128:(cc + 1) * 128], identS[:, :]))(
                                s, cc, pslot),
                         reads=[bTM[s], bID], writes=[pbuf])
                    emit('scalar', (lambda i, cc, pslot: lambda e: e.activation(
                            out=xn[:, cc, i * 128:(i + 1) * 128], in_=pslot,
                            func=AF.Identity))(i, cc, pslot),
                         reads=[pbuf], writes=[bXN])
                emit_gate(i)

            if b == 0:
                ev_xcopy = emit('sync',
                                lambda e: e.dma_start(out=d_out[:, :], in_=d_x[:, :]),
                                writes=[bOUT])

            for cc in range(8):
                a = cacc[cc % 2]
                ab = bCA[cc % 2]
                emit('vector', (lambda cc, a: lambda e: e.tensor_scalar(
                        out=a[:, 0:NK], in0=xn[:, cc, 0:seq:2], scalar1=cw[:, cc, 1:2],
                        scalar2=cb[:, cc:cc + 1], op0=ALU.mult, op1=ALU.add))(cc, a),
                     reads=[bXN, bCW, bCB], writes=[ab])
                emit('vector', (lambda cc, a: lambda e: e.scalar_tensor_tensor(
                        out=a[:, 0:NK], in0=xn[:, cc, 1:seq:2], scalar=cw[:, cc, 2:3],
                        in1=a[:, 0:NK], op0=ALU.mult, op1=ALU.add))(cc, a),
                     reads=[bXN, bCW, ab], writes=[ab])
                emit('vector', (lambda cc, a: lambda e: e.scalar_tensor_tensor(
                        out=regA[:, cc, 1:NK], in0=xn[:, cc, 1:seq - 2:2],
                        scalar=cw[:, cc, 0:1], in1=a[:, 1:NK],
                        op0=ALU.mult, op1=ALU.add))(cc, a),
                     reads=[bXN, bCW, ab], writes=[bA])
                emit('vector', (lambda cc, a: lambda e: e.tensor_copy(
                        out=regA[:, cc, 0:1], in_=a[:, 0:1]))(cc, a),
                     reads=[ab], writes=[bA])

            emit('sync', lambda e: e.dma_start(
                    out=W1H[:, :, :], in_=d_awb[1].rearrange("(c p) o -> p c o", p=128)),
                 writes=[bW1], extra=ev_aw[1])
            emit('sync', lambda e: e.dma_start(
                    out=W2H[:, :, :], in_=d_awb[2].rearrange("(c p) o -> p c o", p=128)),
                 writes=[bW2], extra=ev_aw[2])
            nkt = (NK + 511) // 512
            for oc in range(8):
                for kt in range(nkt):
                    k0 = kt * 512
                    kw_ = min(512, NK - k0)
                    pb, pbuf = [(pF, bpF), (pE, bpE)][(oc * nkt + kt) % 2]
                    emit('tensor', (lambda oc, k0, kw_, pb: lambda e: [
                            e.matmul(pb[:, 0:kw_],
                                     W1H[:, cc, oc * 128:(oc + 1) * 128],
                                     regA[:, cc, k0:k0 + kw_],
                                     start=(cc == 0), stop=(cc == 7))
                            for cc in range(8)][-1])(oc, k0, kw_, pb),
                         reads=[bW1, bA], writes=[pbuf])
                    emit('scalar', (lambda oc, k0, kw_, pb: lambda e: e.activation(
                            out=kf[:, oc, k0:k0 + kw_], in_=pb[:, 0:kw_],
                            func=AF.Identity, bias=kb[:, oc:oc + 1]))(oc, k0, kw_, pb),
                         reads=[pbuf, bKB], writes=[bKF])
            for kc in range(NKB):
                for ocn in range(2):
                    pb, pbuf = [(pF, bpF), (pE, bpE)][(kc * 2 + ocn) % 2]
                    def vmm(e, kc=kc, ocn=ocn, pb=pb):
                        for cc in range(8):
                            e.matmul(pb[:, 0:512], regA[:, cc, kc * 128:(kc + 1) * 128],
                                     W2H[:, cc, ocn * 512:(ocn + 1) * 512],
                                     start=(cc == 0), stop=False)
                        return e.matmul(pb[:, 0:512], onesB[0:1, :],
                                        vbrow[0:1, ocn * 512:(ocn + 1) * 512],
                                        start=False, stop=True)
                    emit('tensor', vmm, reads=[bA, bW2, bONEB, bVB], writes=[pbuf])
                    emit('vector', (lambda kc, ocn, pb: lambda e: e.tensor_copy(
                            out=vpx[:, kc, :].rearrange("p (h c) -> p h c", c=65)
                                [:, ocn * 8:(ocn + 1) * 8, 0:64],
                            in_=pb[:, 0:512].rearrange("p (h c) -> p h c", c=64)))(
                                kc, ocn, pb),
                         reads=[pbuf], writes=[bVPX])

            for Hh in range(NHALF):
                ht0 = Hh * HALF

                for e_i in range(NE):
                    for uh in range(2):
                        emit('sync', (lambda e_i, uh: lambda e: e.dma_start(
                                out=W1H[:, :, :],
                                in_=d_w1b[e_i, :, uh * 1024:(uh + 1) * 1024]
                                .rearrange("(c p) h -> p c h", p=128)))(e_i, uh),
                             writes=[bW1], extra=ev_w1p[e_i][uh])
                        emit('sync', (lambda e_i, uh: lambda e: e.dma_start(
                                out=W2H[:, :, :],
                                in_=d_w2b[e_i, uh * 1024:(uh + 1) * 1024, :]
                                .rearrange("(h p) d -> p h d", p=128)))(e_i, uh),
                             writes=[bW2], extra=ev_w2p[e_i][uh])
                        for tsub in range(NCH):
                            t0 = ht0 + tsub * 512
                            for uc in range(8):
                                ph, phb = [(pSC0[:, 0:512], bS0a),
                                           (pSC0[:, 512:1024], bS0b)][uc % 2]
                                emit('tensor', (lambda uc, t0, ph: lambda e: [
                                        e.matmul(ph,
                                                 W1H[:, cc, uc * 128:(uc + 1) * 128],
                                                 xn[:, cc, t0:t0 + 512],
                                                 start=(cc == 0), stop=(cc == 7))
                                        for cc in range(8)][-1])(uc, t0, ph),
                                     reads=[bW1, bXN], writes=[phb])
                                emit('scalar', (lambda uc, uh, e_i, ph, tsub=tsub:
                                        lambda e: e.activation(
                                            out=regA[:, uc,
                                                     tsub * 512:(tsub + 1) * 512],
                                            in_=ph, func=AF.Gelu,
                                            bias=b1T[:, uh * 8 + uc, e_i:e_i + 1])
                                        )(uc, uh, e_i, ph),
                                     reads=[phb, bB1T], writes=[bA])
                            if e_i == NE - 1 and uh == 1 and tsub == NCH - 1:
                                emit('sync', lambda e: e.dma_start(
                                        out=W1H[:, :, :],
                                        in_=d_awb[0].rearrange("(c p) o -> p c o",
                                                               p=128)),
                                     writes=[bW1], extra=ev_aw[0])
                            for tl in range(4):
                                tg = tsub * 4 + tl
                                for ocn in range(2):
                                    py, pyb = [(pC, bpC), (pD, bpD)][ocn]
                                    addb2 = (e_i == NE - 1 and uh == 1)
                                    def ymm(e, tl=tl, ocn=ocn, py=py, e_i=e_i,
                                            uh=uh, tsub=tsub, addb2=addb2):
                                        last = None
                                        for uc in range(8):
                                            c0 = tsub * 512 + tl * 128
                                            last = e.matmul(
                                                py[:, :],
                                                regA[:, uc, c0:c0 + 128],
                                                W2H[:, uc, ocn * 512:(ocn + 1) * 512],
                                                start=(uc == 0),
                                                stop=(uc == 7 and not addb2))
                                        if addb2:
                                            last = e.matmul(
                                                py[:, :], onesB[0:1, :],
                                                rowA[0:1, ocn * 512:(ocn + 1) * 512],
                                                start=False, stop=True)
                                        return last
                                    rd = [bA, bW2] + ([bONEB, bRowA] if addb2 else [])
                                    emit('tensor', ymm, reads=rd, writes=[pyb])
                                    if e_i == 0 and uh == 0:
                                        def yev(e, tg=tg, ocn=ocn, py=py):
                                            return e.tensor_copy(
                                                out=yb[:, tg, ocn * 512:(ocn + 1) * 512],
                                                in_=py[:, :])
                                    else:
                                        def yev(e, tg=tg, ocn=ocn, py=py):
                                            return e.scalar_tensor_tensor(
                                                out=yb[:, tg, ocn * 512:(ocn + 1) * 512],
                                                in0=py[:, :], scalar=1.0,
                                                in1=yb[:, tg, ocn * 512:(ocn + 1) * 512],
                                                op0=ALU.mult, op1=ALU.add)
                                    emit('vector', yev, reads=[pyb, bY], writes=[bY])
                                    if e_i == NE - 1 and uh == 1:
                                        tcol = (ht0 // 128) + tg
                                        emit('scalar', (lambda tg=tg, ocn=ocn,
                                                        tcol=tcol: lambda e:
                                                e.activation(
                                                    out=yb[:, tg,
                                                           ocn * 512:(ocn + 1) * 512],
                                                    in_=yb[:, tg,
                                                           ocn * 512:(ocn + 1) * 512],
                                                    func=AF.Silu,
                                                    scale=ng[:, tcol:tcol + 1]))(),
                                             reads=[bY, bNG], writes=[bY])

                if debug and b == 0 and Hh == 0 and "y" in dbg:
                    emit('gpsimd', lambda e: e.dma_start(
                            out=dbg["y"][:, :],
                            in_=yb.rearrange("p a d -> p (a d)")[:, 0:NTG * D]),
                         reads=[bY], writes=[])

                emit('sync', lambda e: e.dma_start(
                        out=W2H[:, :, :], in_=d_awb[3].rearrange("(c p) o -> p c o", p=128)),
                     writes=[bW2], extra=ev_aw[3])

                for qt in range(NCH):
                    q0 = ht0 + qt * 512
                    for oc in range(8):
                        pb, pbuf = [(pF, bpF), (pE, bpE)][oc % 2]
                        emit('tensor', (lambda oc, q0, pb: lambda e: [
                                e.matmul(pb[:, :],
                                         W1H[:, cc, oc * 128:(oc + 1) * 128],
                                         xn[:, cc, q0:q0 + 512],
                                         start=(cc == 0), stop=(cc == 7))
                                for cc in range(8)][-1])(oc, q0, pb),
                             reads=[bW1, bXN], writes=[pbuf])
                        emit('scalar', (lambda oc, pb: lambda e: e.activation(
                                out=qt_sb[:, oc, :], in_=pb[:, :], func=AF.Identity,
                                bias=qb[:, oc:oc + 1]))(oc, pb),
                             reads=[pbuf, bQB], writes=[bQT[oc]])

                    NKP = NKB // 2
                    for pr2 in range(8):
                        for kp in range(NKP):
                            def scm(e, pr2=pr2, kp=kp):
                                last = None
                                for half in range(2):
                                    kcc = 2 * kp + half
                                    col = slice(half * 512, half * 512 + 512)
                                    e.matmul(pSC0[:, col],
                                             kf[0:64, pr2, kcc * 128:(kcc + 1) * 128],
                                             qt_sb[0:64, pr2, :],
                                             start=True, stop=True)
                                    last = e.matmul(
                                        pSC1[:, col],
                                        kf[64:128, pr2, kcc * 128:(kcc + 1) * 128],
                                        qt_sb[64:128, pr2, :],
                                        start=True, stop=True)
                                return last
                            emit('tensor', scm, reads=[bKF, bQT[pr2]],
                                 writes=[bS0a, bS0b, bS1a, bS1b])
                            emit('scalar', (lambda kp: lambda e: e.activation(
                                    out=Xt[0][:, :], in_=pSC0[:, :], func=AF.Exp,
                                    scale=SCALE_ATT))(kp),
                                 reads=[bS0a, bS0b], writes=[bXt[0]])
                            emit('scalar', (lambda kp: lambda e: e.activation(
                                    out=Xt[1][:, :], in_=pSC1[:, :], func=AF.Exp,
                                    scale=SCALE_ATT))(kp),
                                 reads=[bS1a, bS1b], writes=[bXt[1]])
                            for hp in range(2):
                                h = 2 * pr2 + hp
                                pav, pavb = [(pC, bpC), (pD, bpD)][hp]
                                def avm(e, h=h, kp=kp, pav=pav, hp=hp):
                                    e.matmul(pav[0:65, :],
                                             vpx[:, 2 * kp, h * 65:(h + 1) * 65],
                                             Xt[hp][:, 0:512], start=(kp == 0),
                                             stop=False)
                                    return e.matmul(
                                        pav[0:65, :],
                                        vpx[:, 2 * kp + 1, h * 65:(h + 1) * 65],
                                        Xt[hp][:, 512:1024],
                                        start=False, stop=(kp == NKP - 1))
                                emit('tensor', avm, reads=[bVPX, bXt[hp]],
                                     writes=[pavb])
                        for hp, (pav, pavb) in enumerate([(pC, bpC), (pD, bpD)]):
                            emit('scalar', (lambda pav: lambda e: e.activation(
                                    out=rtmp[:, :], in_=pav[64:65, :],
                                    func=AF.Ln))(pav),
                                 reads=[pavb], writes=[bRT])
                            emit('scalar', (lambda hp: lambda e: e.activation(
                                    out=recips[0:1, hp * 512:(hp + 1) * 512],
                                    in_=rtmp[:, :], func=AF.Exp, scale=-1.0))(hp),
                                 reads=[bRT], writes=[bREC])
                        def rbm(e):
                            e.matmul(pE[:, :], selA[0:1, :], recips[0:1, 0:512],
                                     start=True, stop=False)
                            return e.matmul(pE[:, :], selB[0:1, :],
                                            recips[0:1, 512:1024],
                                            start=False, stop=True)
                        emit('tensor', rbm, reads=[bSelA, bSelB, bREC], writes=[bpE])
                        rbs = pr2 % 2
                        emit('scalar', (lambda rbs: lambda e: e.activation(
                                out=RB[rbs][:, :], in_=pE[:, :],
                                func=AF.Identity))(rbs),
                             reads=[bpE], writes=[bRB[rbs]])
                        emit('vector', (lambda pr2, rbs: lambda e: e.tensor_tensor(
                                out=attn_o[0:64, pr2, :], in0=pC[0:64, :],
                                in1=RB[rbs][0:64, :], op=ALU.mult))(pr2, rbs),
                             reads=[bpC, bRB[rbs]], writes=[bAO])
                        emit('vector', (lambda pr2, rbs: lambda e: e.tensor_tensor(
                                out=attn_o[64:128, pr2, :], in0=pD[0:64, :],
                                in1=RB[rbs][64:128, :], op=ALU.mult))(pr2, rbs),
                             reads=[bpD, bRB[rbs]], writes=[bAO])

                    for qq in range(4):
                        tg = qt * 4 + qq
                        for ocn in range(2):
                            pb, pbuf = [(pF, bpF), (pE, bpE)][ocn]
                            def smm(e, qq=qq, ocn=ocn, pb=pb):
                                for cc in range(8):
                                    e.matmul(pb[:, :], attn_o[:, cc, qq * 128:(qq + 1) * 128],
                                             W2H[:, cc, ocn * 512:(ocn + 1) * 512],
                                             start=(cc == 0), stop=False)
                                return e.matmul(pb[:, :], onesB[0:1, :],
                                                obrow[0:1, ocn * 512:(ocn + 1) * 512],
                                                start=False, stop=True)
                            emit('tensor', smm, reads=[bAO, bW2, bONEB, bOB],
                                 writes=[pbuf])
                            sgs = ocn
                            emit('vector', (lambda tg, ocn, pb, sgs: lambda e:
                                    e.tensor_tensor(
                                        out=sg_sl[sgs][:, :], in0=pb[:, :],
                                        in1=yb[:, tg, ocn * 512:(ocn + 1) * 512],
                                        op=ALU.mult))(tg, ocn, pb, sgs),
                                 reads=[pbuf, bY], writes=[bSG[sgs]])
                            r0 = brow + ht0 + tg * 128
                            emit('gpsimd', (lambda r0, ocn, sgs: lambda e: e.dma_start(
                                    out=d_out[r0:r0 + 128, ocn * 512:(ocn + 1) * 512],
                                    in_=sg_sl[sgs][:, :], accum_op=ALU.add))(r0, ocn, sgs),
                                 reads=[bSG[sgs]], extra=[ev_xcopy])

            if debug and b == 0:
                if "xn" in dbg:
                    emit('gpsimd', lambda e: e.dma_start(
                            out=dbg["xn"][:, :],
                            in_=xn.rearrange("p a s -> p (a s)")[:, :]),
                         reads=[bXN])
                if "ng" in dbg:
                    emit('gpsimd', lambda e: e.dma_start(out=dbg["ng"][:, :], in_=ng[:, :]),
                         reads=[bNG])
                if "kf" in dbg:
                    emit('gpsimd', lambda e: e.dma_start(
                            out=dbg["kf"][:, :],
                            in_=kf.rearrange("p a s -> p (a s)")[:, :]),
                         reads=[bKF])
                if "vpx" in dbg:
                    emit('gpsimd', lambda e: e.dma_start(
                            out=dbg["vpx"][:, :],
                            in_=vpx.rearrange("p a s -> p (a s)")[:, :]),
                         reads=[bVPX])

        fin = [(s, cnt[id(s)]) for s in LS + LG + [PE, SC, VE] if cnt[id(s)] > 0]
        items.append(('sync', fin, lambda e: None, None, 0))

        waited = {}
        for (_, ws, _, _, _) in items:
            for (s, v) in ws:
                waited.setdefault(id(s), set()).add(v)
        unit = {}
        for s in LS + LG:
            unit[id(s)] = 16
        for s in (PE, SC, VE):
            unit[id(s)] = 1
        rank = {sid: {v: (i + 1) for i, v in enumerate(sorted(vals))}
                for sid, vals in waited.items()}
        hw = {}

        def body(eng, name):
            for (e, ws, fn, sem, evv) in items:
                if e != name:
                    continue
                for (s, v) in ws:
                    nv = rank[id(s)][v] * unit[id(s)]
                    key = (name, id(s))
                    if hw.get(key, -1) >= nv:
                        continue
                    hw[key] = nv
                    eng.wait_ge(s, nv)
                ins = fn(eng)
                if ins is not None and sem is not None \
                        and evv in waited.get(id(sem), ()):
                    ins.then_inc(sem, unit[id(sem)])

        block.sync(lambda e: body(e, 'sync'))
        block.gpsimd(lambda e: body(e, 'gpsimd'))
        block.tensor(lambda e: body(e, 'tensor'))
        block.scalar(lambda e: body(e, 'scalar'))
        block.vector(lambda e: body(e, 'vector'))

    return nc



def host_layout(inputs):
    f = np.ascontiguousarray
    a = lambda k: np.asarray(inputs[k], np.float32)
    common = dict(
        qwT=f(a("L0_attn_q_w").T), kwT=f(a("L0_attn_k_w").T),
        vwT=f(a("L0_attn_v_w").T), owT=f(a("L0_attn_out_w").T),
        w1=f(a("L0_ffn_w1")), w2=f(a("L0_ffn_w2")),
        gwT=f(a("L0_ffn_gate_w").T),
        b1T=f(a("L0_ffn_b1").T.reshape(16, 128, NE).transpose(1, 0, 2)
              .reshape(128, 16 * NE)),
        b2=f(a("L0_ffn_b2")),
        qb=f(a("L0_attn_q_b").reshape(8, 128).T),
        kb=f(a("L0_attn_k_b").reshape(8, 128).T),
        vb=f(a("L0_attn_v_b")[None, :]), ob=f(a("L0_attn_out_b")[None, :]),
        gb=f(a("L0_ffn_gate_b")[None, :]), rw=f(a("L0_ffn_rms_w")[None, :]),
        cw=f(a("L0_attn_conv_w")[:, 0, :].reshape(8, 128, 3).transpose(1, 0, 2)
             .reshape(128, 24)),
        cb=f(a("L0_attn_conv_b").reshape(8, 128).T),
        ident=np.eye(128, dtype=np.float32),
    )
    return common


def kernel(**inputs):
    x = np.asarray(inputs["L0_attn_x"], np.float32)
    bpc = x.shape[0] // NCORES
    seq = x.shape[1]
    common = host_layout(inputs)
    nc = build_program(seq=seq, bpc=bpc, debug=False)
    in_maps = []
    for c in range(NCORES):
        m = dict(common)
        m["x"] = np.ascontiguousarray(
            x[c * bpc:(c + 1) * bpc].reshape(bpc * seq, D))
        in_maps.append(m)
    res = run_bass_kernel_spmd(nc, in_maps, core_ids=list(range(NCORES)))
    return np.concatenate(
        [r["out"].reshape(bpc, seq, D) for r in res.results], axis=0)


# version 22
# speedup vs baseline: 1.0661x; 1.0661x over previous
import sys
sys.path.insert(0, '/opt/trn_rl_repo')
import numpy as np
import concourse.bass as bass
import concourse.mybir as mybir
from concourse.bass_utils import run_bass_kernel_spmd
from contextlib import ExitStack

F32 = mybir.dt.float32
BF16 = mybir.dt.bfloat16
AF = mybir.ActivationFunctionType
ALU = mybir.AluOpType
AX = mybir.AxisListType

NCORES = 8
D = 1024
NE = 8
NH = 16
EPS_LN = 1e-5
EPS_RMS = 1.1920929e-07
SCALE_ATT = 0.125


class Buf:
    __slots__ = ("w", "rs", "name")

    def __init__(self, name):
        self.name = name
        self.w = None
        self.rs = []


def build_program(seq=2048, bpc=2, debug=False):
    NT = seq // 128
    NK = seq // 2
    NKB = NK // 128
    HALF = min(1024, seq)
    NHALF = seq // HALF
    NCH = HALF // 512
    NTG = HALF // 128

    nc = bass.Bass("TRN2", target_bir_lowering=False, debug=False)

    d_x = nc.dram_tensor("x", [bpc * seq, D], F32, kind="ExternalInput").ap()
    d_qwT = nc.dram_tensor("qwT", [D, D], F32, kind="ExternalInput").ap()
    d_kwT = nc.dram_tensor("kwT", [D, D], F32, kind="ExternalInput").ap()
    d_vwT = nc.dram_tensor("vwT", [D, D], F32, kind="ExternalInput").ap()
    d_owT = nc.dram_tensor("owT", [D, D], F32, kind="ExternalInput").ap()
    d_w1 = nc.dram_tensor("w1", [NE, D, 2048], F32, kind="ExternalInput").ap()
    d_w2 = nc.dram_tensor("w2", [NE, 2048, D], F32, kind="ExternalInput").ap()
    d_gwT = nc.dram_tensor("gwT", [D, NE], F32, kind="ExternalInput").ap()
    d_b1T = nc.dram_tensor("b1T", [128, 16 * NE], F32, kind="ExternalInput").ap()
    d_b2 = nc.dram_tensor("b2", [NE, D], F32, kind="ExternalInput").ap()
    d_qb = nc.dram_tensor("qb", [128, 8], F32, kind="ExternalInput").ap()
    d_kb = nc.dram_tensor("kb", [128, 8], F32, kind="ExternalInput").ap()
    d_vb = nc.dram_tensor("vb", [1, D], F32, kind="ExternalInput").ap()
    d_ob = nc.dram_tensor("ob", [1, D], F32, kind="ExternalInput").ap()
    d_gb = nc.dram_tensor("gb", [1, NE], F32, kind="ExternalInput").ap()
    d_rw = nc.dram_tensor("rw", [1, NE], F32, kind="ExternalInput").ap()
    d_cw = nc.dram_tensor("cw", [128, 24], F32, kind="ExternalInput").ap()
    d_cb = nc.dram_tensor("cb", [128, 8], F32, kind="ExternalInput").ap()
    d_ident = nc.dram_tensor("ident", [128, 128], F32, kind="ExternalInput").ap()
    d_w1b = nc.dram_tensor("w1b", [NE, D, 2048], BF16, kind="Internal").ap()
    d_w2b = nc.dram_tensor("w2b", [NE, 2048, D], BF16, kind="Internal").ap()
    d_awb = nc.dram_tensor("awb", [4, D, D], BF16, kind="Internal").ap()
    d_b2b = nc.dram_tensor("b2b", [NE, D], BF16, kind="Internal").ap()
    d_out = nc.dram_tensor("out", [bpc * seq, D], F32, kind="ExternalOutput").ap()
    dbg = {}
    if debug:
        dbg["xn"] = nc.dram_tensor("dbg_xn", [128, 8 * seq], F32, kind="ExternalOutput").ap()
        dbg["ng"] = nc.dram_tensor("dbg_ng", [128, NT], F32, kind="ExternalOutput").ap()
        dbg["kf"] = nc.dram_tensor("dbg_kf", [128, 8 * NK], F32, kind="ExternalOutput").ap()
        dbg["vpx"] = nc.dram_tensor("dbg_vpx", [128, NKB * 1040], F32, kind="ExternalOutput").ap()
        dbg["y"] = nc.dram_tensor("dbg_y", [128, NTG * D], F32, kind="ExternalOutput").ap()

    with ExitStack() as es:
        sb = lambda name, shape, dt: es.enter_context(nc.sbuf_tensor(name, shape, dt))
        ps = lambda name, shape, dt: es.enter_context(nc.psum_tensor(name, shape, dt))

        xn = sb("xn", [128, 8, seq], BF16)
        yb = sb("yb", [128, NTG, D], F32)
        kf = sb("kf", [128, 8, NK], BF16)
        vpx = sb("vpx", [128, NKB, 1040], BF16)
        W1H = sb("W1H", [128, 8, 1024], BF16)
        W2H = sb("W2H", [128, 8, 1024], BF16)
        regA = sb("regA", [128, 8, 1024], BF16)
        xt_sl = [sb(f"xt{i}", [128, D], F32) for i in range(2)]
        xn_tm = [sb(f"xntm{i}", [128, D], BF16) for i in range(2)]
        qt_sb = sb("qt", [128, 8, 512], BF16)
        Xt = [sb(f"Xt{i}", [128, 1024], BF16) for i in range(2)]
        attn_o = sb("attn_o", [128, 8, 512], BF16)
        RB = [sb(f"RB{i}", [128, 512], F32) for i in range(2)]
        sg_sl = [sb(f"sg{i}", [128, 512], F32) for i in range(2)]
        recips = sb("recips", [1, 2 * 512], BF16)
        rtmp = sb("rtmp", [1, 512], F32)
        junk = sb("junk", [128, D], BF16)
        onesF = sb("onesF", [1, 128], F32)
        onesB = sb("onesB", [1, 128], BF16)
        ones8 = sb("ones8", [8, 1], BF16)
        selA = sb("selA", [1, 128], BF16)
        selB = sb("selB", [1, 128], BF16)
        identS = sb("identS", [128, 128], BF16)
        gw = sb("gws", [128, 8, NE], BF16)
        b1T = sb("b1Ts", [128, 16, NE], F32)
        qb = sb("qbs", [128, 8], F32)
        kb = sb("kbs", [128, 8], F32)
        cw = sb("cws", [128, 8, 3], F32)
        cb = sb("cbs", [128, 8], F32)
        gbr = sb("gbr", [1, NE], F32)
        rwr = sb("rwr", [1, NE], F32)
        rowA = sb("rowA", [1, D], BF16)
        vbrow = sb("vbrow", [1, D], BF16)
        obrow = sb("obrow", [1, D], BF16)
        GB = sb("GB", [128, NE], F32)
        RW2 = sb("RW2", [128, NE], F32)
        ng = sb("ng", [128, NT], F32)
        lnst = sb("lnst", [128, 2, 8], F32)
        gst = sb("gst", [128, 2, 8], F32)
        gsc = sb("gsc", [128, 2, NE], F32)
        gz2 = sb("gz2", [128, 2, NE], F32)
        junk8 = sb("junk8", [128, NE], F32)
        epsc = sb("epsc", [128, 4], F32)
        cacc = [sb(f"cacc{i}", [128, NK], F32) for i in range(2)]

        pSC0 = ps("pSC0", [128, 1024], F32)
        pSC1 = ps("pSC1", [128, 1024], F32)
        pC = ps("pC", [128, 512], F32)
        pD = ps("pD", [128, 512], F32)
        pE = ps("pE", [128, 512], F32)
        pF = ps("pF", [128, 512], F32)
        pSC1b = pSC1.bitcast(BF16)
        trslot = [pSC1b[:, 0:128], pSC1b[:, 1024:1152]]

        LS = [es.enter_context(nc.semaphore(f"LS{i}")) for i in range(2)]
        LG = [es.enter_context(nc.semaphore(f"LG{i}")) for i in range(2)]
        PE = es.enter_context(nc.semaphore("PE"))
        SC = es.enter_context(nc.semaphore("SC"))
        VE = es.enter_context(nc.semaphore("VE"))
        block = es.enter_context(nc.Block())

        cnt = {}
        for s in LS + LG + [PE, SC, VE]:
            cnt[id(s)] = 0
        dma_sems = {'sync': LS, 'gpsimd': LG}
        dma_idx = {'sync': 0, 'gpsimd': 0}
        eng_sem = {'tensor': PE, 'scalar': SC, 'vector': VE}
        items = []

        def emit(eng, fn, reads=(), writes=(), extra=()):
            waits = []
            for b in reads:
                if b.w is not None:
                    waits.append(b.w)
            for b in writes:
                if b.w is not None:
                    waits.append(b.w)
                waits.extend(b.rs)
            waits.extend(e for e in extra if e is not None)
            if eng in dma_sems:
                sem = dma_sems[eng][dma_idx[eng] % 2]
                dma_idx[eng] += 1
                if cnt[id(sem)] > 0:
                    waits.append((sem, cnt[id(sem)]))
                cnt[id(sem)] += 16
            else:
                sem = eng_sem[eng]
                cnt[id(sem)] += 1
            ev = (sem, cnt[id(sem)])
            items.append((eng, waits, fn, sem, ev[1]))
            for b in reads:
                b.rs.append(ev)
            for b in writes:
                b.w = ev
                b.rs = []
            return ev

        bXN = Buf("xn"); bY = Buf("y"); bKF = Buf("kf"); bVPX = Buf("vpx")
        bW1 = Buf("W1H"); bW2 = Buf("W2H"); bA = Buf("regA")
        bXT = [Buf("xt0"), Buf("xt1")]; bTM = [Buf("tm0"), Buf("tm1")]
        bQT = [Buf(f"qt{i}") for i in range(8)]
        bXt = [Buf("Xt0"), Buf("Xt1")]
        bAO = Buf("attn_o"); bRB = [Buf("RB0"), Buf("RB1")]
        bSG = [Buf("sg0"), Buf("sg1")]; bREC = Buf("recips")
        bJ = Buf("junk"); bJ8 = Buf("junk8"); bRT = Buf("rtmp")
        bONE = Buf("onesF"); bONEB = Buf("onesB"); bONE8 = Buf("ones8"); bSelA = Buf("selA"); bSelB = Buf("selB"); bID = Buf("ident")
        bGW = Buf("gw"); bB1T = Buf("b1T"); bQB = Buf("qb"); bKB = Buf("kb")
        bCW = Buf("cw"); bCB = Buf("cb"); bGBr = Buf("gbr"); bRWr = Buf("rwr")
        bRowA = Buf("rowA"); bVB = Buf("vbrow"); bOB = Buf("obrow")
        bGB = Buf("GB"); bRW2 = Buf("RW2")
        bNG = Buf("ng"); bLN = Buf("lnst"); bGS = Buf("gst")
        bGSC = [Buf("gsc0"), Buf("gsc1")]; bGZ = [Buf("gz0"), Buf("gz1")]
        bEPS = Buf("epsc"); bCA = [Buf("cacc0"), Buf("cacc1")]
        bS0a = Buf("pSC0a"); bS0b = Buf("pSC0b")
        bS1a = Buf("pSC1a"); bS1b = Buf("pSC1b")
        bpC = Buf("pC"); bpD = Buf("pD"); bpE = Buf("pE"); bpF = Buf("pF")
        bOUT = Buf("d_out")

        ev_xcopy = None
        emit('gpsimd', lambda e: e.dma_start(out=gw[:, :, :],
                                             in_=d_gwT.rearrange("(c p) e -> p c e", p=128)),
             writes=[bGW])
        emit('gpsimd', lambda e: e.dma_start(out=identS[:, :], in_=d_ident[:, :]),
             writes=[bID])
        for dst, src, b in [
            (b1T[:, :, :], d_b1T.rearrange("p (c e) -> p c e", e=NE), bB1T),
            (qb[:, :], d_qb[:, :], bQB), (kb[:, :], d_kb[:, :], bKB),
            (cw[:, :, :], d_cw.rearrange("p (c t) -> p c t", t=3), bCW),
            (cb[:, :], d_cb[:, :], bCB),
            (gbr[:, :], d_gb[:, :], bGBr), (rwr[:, :], d_rw[:, :], bRWr),
        ]:
            emit('sync', (lambda ds, sr: lambda e: e.dma_start(out=ds, in_=sr))(dst, src),
                 writes=[b])
        emit('vector', lambda e: e.memset(onesF[:, :], 1.0), writes=[bONE])
        emit('vector', lambda e: e.memset(onesB[:, :], 1.0), writes=[bONEB])
        emit('vector', lambda e: e.memset(ones8[:, :], 1.0), writes=[bONE8])
        emit('vector', lambda e: e.memset(epsc[:, 0:1], EPS_LN), writes=[bEPS])
        emit('vector', lambda e: e.memset(epsc[:, 1:2], EPS_RMS), reads=[], writes=[bEPS])
        emit('vector', lambda e: e.memset(selA[:, :], 0.0), writes=[bSelA])
        emit('vector', lambda e: e.memset(selA[0:1, 0:64], 1.0), writes=[bSelA])
        emit('vector', lambda e: e.memset(selB[:, :], 0.0), writes=[bSelB])
        emit('vector', lambda e: e.memset(selB[0:1, 64:128], 1.0), writes=[bSelB])
        for kc in range(NKB):
            emit('vector',
                 (lambda kc: lambda e: e.memset(
                     vpx[:, kc, :].rearrange("p (h c) -> p h c", c=65)[:, :, 64:65], 1.0))(kc),
                 writes=[bVPX])
        emit('tensor', lambda e: e.matmul(pC[:, 0:NE], onesF[0:1, :], gbr[0:1, :],
                                          start=True, stop=True),
             reads=[bONE, bGBr], writes=[bpC])
        emit('vector', lambda e: e.tensor_copy(out=GB[:, :], in_=pC[:, 0:NE]),
             reads=[bpC], writes=[bGB])
        emit('tensor', lambda e: e.matmul(pC[:, 0:NE], onesF[0:1, :], rwr[0:1, :],
                                          start=True, stop=True),
             reads=[bONE, bRWr], writes=[bpC])
        emit('vector', lambda e: e.tensor_scalar(out=RW2[:, :], in0=pC[:, 0:NE],
                                                 scalar1=2.0, scalar2=None, op0=ALU.mult),
             reads=[bpC], writes=[bRW2])

        emit('gpsimd', lambda e: e.dma_start(out=vbrow[:, :], in_=d_vb[:, :]),
             writes=[bVB])
        emit('gpsimd', lambda e: e.dma_start(out=obrow[:, :], in_=d_ob[:, :]),
             writes=[bOB])

        stg = [(attn_o, [bAO]), (qt_sb, bQT)]
        ev_aw = [None] * 4
        ev_w1p = [[[], []] for _ in range(NE)]
        ev_w2p = [[[], []] for _ in range(NE)]
        pp_i = 0

        def prepass(src_ap, dst_ap, wide=False):
            nonlocal pp_i
            st, stb = stg[pp_i % 2]
            pp_i += 1
            stv = st.rearrange("p (x y) c -> p x (y c)", y=2) if wide else st
            emit('gpsimd', (lambda stv, src_ap: lambda e: e.dma_start(
                    out=stv[:, :, :], in_=src_ap))(stv, src_ap), writes=stb)
            return emit('gpsimd', (lambda stv, dst_ap: lambda e: e.dma_start(
                    out=dst_ap, in_=stv[:, :, :]))(stv, dst_ap), reads=stb)

        qfl = qt_sb.rearrange("p a c -> p (a c)")
        emit('gpsimd', lambda e: e.dma_start(out=qfl[0:NE, 0:D], in_=d_b2[:, :]),
             writes=bQT)
        for hf in range(2):
            emit('tensor', (lambda hf: lambda e: e.matmul(
                    pC[:1, 0:512], ones8[:, :],
                    qfl[0:NE, hf * 512:(hf + 1) * 512],
                    start=True, stop=True))(hf),
                 reads=bQT + [bONE8], writes=[bpC])
            emit('scalar', (lambda hf: lambda e: e.activation(
                    out=rowA[0:1, hf * 512:(hf + 1) * 512], in_=pC[:1, 0:512],
                    func=AF.Identity))(hf),
                 reads=[bpC], writes=[bRowA])

        for i, dsrc in enumerate([d_qwT, d_kwT, d_vwT, d_owT]):
            evs = []
            for q in range(2):
                evs.append(prepass(
                    dsrc[:, q * 512:(q + 1) * 512].rearrange("(c p) o -> p c o", p=128),
                    d_awb[i, :, q * 512:(q + 1) * 512].rearrange("(c p) o -> p c o", p=128)))
            ev_aw[i] = evs
        for e_i in range(NE):
            for uq in range(4):
                ev = prepass(
                    d_w1[e_i, :, uq * 512:(uq + 1) * 512]
                    .rearrange("(c p) h -> p c h", p=128),
                    d_w1b[e_i, :, uq * 512:(uq + 1) * 512]
                    .rearrange("(c p) h -> p c h", p=128))
                ev_w1p[e_i][uq // 2].append(ev)
            for uq in range(4):
                ev = prepass(
                    d_w2[e_i, uq * 512:(uq + 1) * 512, :]
                    .rearrange("(a p) d -> p a d", p=128),
                    d_w2b[e_i, uq * 512:(uq + 1) * 512, :]
                    .rearrange("(a p) d -> p a d", p=128), wide=True)
                ev_w2p[e_i][uq // 2].append(ev)

        for b in range(bpc):
            brow = b * seq

            def gs(i, j):
                return gst[:, i % 2, j:j + 1]

            def emit_gate(i):
                s = i % 2
                t0 = i * 128
                emit('tensor', (lambda t0: lambda e: [
                        e.matmul(pC[:, 0:NE], xn[:, cc, t0:t0 + 128], gw[:, cc, :],
                                 start=(cc == 0), stop=(cc == 7))
                        for cc in range(8)][-1])(t0),
                     reads=[bXN, bGW], writes=[bpC])
                emit('vector', (lambda s: lambda e: e.tensor_tensor(
                        out=gsc[:, s, :], in0=pC[:, 0:NE], in1=GB[:, :], op=ALU.add))(s),
                     reads=[bpC, bGB], writes=[bGSC[s]])
                emit('scalar', (lambda s, i: lambda e: e.activation(
                        out=junk8[:, :], in_=gsc[:, s, :], func=AF.Square,
                        accum_out=gs(i, 0)))(s, i),
                     reads=[bGSC[s]], writes=[bJ8, bGS])
                emit('scalar', (lambda i: lambda e: e.activation(
                        out=gs(i, 1), in_=gs(i, 0), func=AF.Ln, scale=1.0 / NE,
                        bias=epsc[:, 1:2]))(i),
                     reads=[bGS, bEPS], writes=[bGS])
                emit('scalar', (lambda i: lambda e: e.activation(
                        out=gs(i, 2), in_=gs(i, 1), func=AF.Exp, scale=-0.5))(i),
                     reads=[bGS], writes=[bGS])
                emit('vector', (lambda s, i: lambda e: e.scalar_tensor_tensor(
                        out=gz2[:, s, :], in0=gsc[:, s, :], scalar=gs(i, 2),
                        in1=RW2[:, :], op0=ALU.mult, op1=ALU.mult))(s, i),
                     reads=[bGSC[s], bGS, bRW2], writes=[bGZ[s]])
                emit('vector', (lambda s, i: lambda e: e.reduce_max(
                        out=gs(i, 3), in_=gz2[:, s, :], axis=AX.X, negate=True))(s, i),
                     reads=[bGZ[s]], writes=[bGS])
                emit('scalar', (lambda s, i: lambda e: e.activation(
                        out=junk8[:, :], in_=gz2[:, s, :], func=AF.Exp, bias=gs(i, 3),
                        accum_out=gs(i, 4)))(s, i),
                     reads=[bGZ[s], bGS], writes=[bJ8, bGS])
                emit('vector', (lambda i: lambda e: e.tensor_scalar(
                        out=gs(i, 5), in0=gs(i, 4), scalar1=1e-6, scalar2=1.0,
                        op0=ALU.mult, op1=ALU.add))(i),
                     reads=[bGS], writes=[bGS])
                emit('vector', (lambda i: lambda e: e.reciprocal(
                        out=ng[:, i:i + 1], in_=gs(i, 5)))(i),
                     reads=[bGS], writes=[bNG])


            def sl(i, j):
                return lnst[:, i % 2, j:j + 1]
            for i in range(NT):
                s = i % 2
                t0 = brow + i * 128
                evld = emit('sync',
                            (lambda s, t0: lambda e: e.dma_start(
                                out=xt_sl[s][:, :], in_=d_x[t0:t0 + 128, :]))(s, t0),
                            writes=[bXT[s]])
                emit('vector', (lambda s, i: lambda e: e.reduce_sum(
                        out=sl(i, 0), in_=xt_sl[s][:, :], axis=AX.X))(s, i),
                     reads=[bXT[s]], writes=[bLN])
                emit('scalar', (lambda s, i: lambda e: e.activation(
                        out=junk[:, :], in_=xt_sl[s][:, :], func=AF.Square,
                        accum_out=sl(i, 1)))(s, i),
                     reads=[bXT[s]], writes=[bJ, bLN])
                emit('vector', (lambda i: lambda e: e.tensor_scalar(
                        out=sl(i, 2), in0=sl(i, 0), scalar1=1.0 / D, scalar2=None,
                        op0=ALU.mult))(i),
                     reads=[bLN], writes=[bLN])
                emit('scalar', (lambda i: lambda e: e.activation(
                        out=sl(i, 3), in_=sl(i, 2), func=AF.Square))(i),
                     reads=[bLN], writes=[bLN])
                emit('vector', (lambda i: lambda e: e.scalar_tensor_tensor(
                        out=sl(i, 4), in0=sl(i, 1), scalar=1.0 / D, in1=sl(i, 3),
                        op0=ALU.mult, op1=ALU.subtract))(i),
                     reads=[bLN], writes=[bLN])
                emit('scalar', (lambda i: lambda e: e.activation(
                        out=sl(i, 5), in_=sl(i, 4), func=AF.Ln, bias=epsc[:, 0:1]))(i),
                     reads=[bLN, bEPS], writes=[bLN])
                emit('scalar', (lambda i: lambda e: e.activation(
                        out=sl(i, 6), in_=sl(i, 5), func=AF.Exp, scale=-0.5))(i),
                     reads=[bLN], writes=[bLN])
                emit('vector', (lambda s, i: lambda e: e.tensor_scalar(
                        out=xn_tm[s][:, :], in0=xt_sl[s][:, :], scalar1=sl(i, 2),
                        scalar2=sl(i, 6), op0=ALU.subtract, op1=ALU.mult))(s, i),
                     reads=[bXT[s], bLN], writes=[bTM[s]])
                for cc in range(8):
                    p = cc % 2
                    pslot = trslot[p]
                    pbuf = [bS1a, bS1b][p]
                    emit('tensor', (lambda s, cc, pslot: lambda e: e.transpose(
                            pslot, xn_tm[s][:, cc * 128:(cc + 1) * 128], identS[:, :]))(
                                s, cc, pslot),
                         reads=[bTM[s], bID], writes=[pbuf])
                    emit('scalar', (lambda i, cc, pslot: lambda e: e.activation(
                            out=xn[:, cc, i * 128:(i + 1) * 128], in_=pslot,
                            func=AF.Identity))(i, cc, pslot),
                         reads=[pbuf], writes=[bXN])
                emit_gate(i)

            if b == 0:
                ev_xcopy = emit('sync',
                                lambda e: e.dma_start(out=d_out[:, :], in_=d_x[:, :]),
                                writes=[bOUT])

            for cc in range(8):
                a = cacc[cc % 2]
                ab = bCA[cc % 2]
                emit('vector', (lambda cc, a: lambda e: e.tensor_scalar(
                        out=a[:, 0:NK], in0=xn[:, cc, 0:seq:2], scalar1=cw[:, cc, 1:2],
                        scalar2=cb[:, cc:cc + 1], op0=ALU.mult, op1=ALU.add))(cc, a),
                     reads=[bXN, bCW, bCB], writes=[ab])
                emit('vector', (lambda cc, a: lambda e: e.scalar_tensor_tensor(
                        out=a[:, 0:NK], in0=xn[:, cc, 1:seq:2], scalar=cw[:, cc, 2:3],
                        in1=a[:, 0:NK], op0=ALU.mult, op1=ALU.add))(cc, a),
                     reads=[bXN, bCW, ab], writes=[ab])
                emit('vector', (lambda cc, a: lambda e: e.scalar_tensor_tensor(
                        out=regA[:, cc, 1:NK], in0=xn[:, cc, 1:seq - 2:2],
                        scalar=cw[:, cc, 0:1], in1=a[:, 1:NK],
                        op0=ALU.mult, op1=ALU.add))(cc, a),
                     reads=[bXN, bCW, ab], writes=[bA])
                emit('vector', (lambda cc, a: lambda e: e.tensor_copy(
                        out=regA[:, cc, 0:1], in_=a[:, 0:1]))(cc, a),
                     reads=[ab], writes=[bA])

            emit('sync', lambda e: e.dma_start(
                    out=W1H[:, :, :], in_=d_awb[1].rearrange("(c p) o -> p c o", p=128)),
                 writes=[bW1], extra=ev_aw[1])
            emit('sync', lambda e: e.dma_start(
                    out=W2H[:, :, :], in_=d_awb[2].rearrange("(c p) o -> p c o", p=128)),
                 writes=[bW2], extra=ev_aw[2])
            nkt = (NK + 511) // 512
            for oc in range(8):
                for kt in range(nkt):
                    k0 = kt * 512
                    kw_ = min(512, NK - k0)
                    pb, pbuf = [(pF, bpF), (pE, bpE)][(oc * nkt + kt) % 2]
                    emit('tensor', (lambda oc, k0, kw_, pb: lambda e: [
                            e.matmul(pb[:, 0:kw_],
                                     W1H[:, cc, oc * 128:(oc + 1) * 128],
                                     regA[:, cc, k0:k0 + kw_],
                                     start=(cc == 0), stop=(cc == 7))
                            for cc in range(8)][-1])(oc, k0, kw_, pb),
                         reads=[bW1, bA], writes=[pbuf])
                    emit('scalar', (lambda oc, k0, kw_, pb: lambda e: e.activation(
                            out=kf[:, oc, k0:k0 + kw_], in_=pb[:, 0:kw_],
                            func=AF.Identity, bias=kb[:, oc:oc + 1]))(oc, k0, kw_, pb),
                         reads=[pbuf, bKB], writes=[bKF])
            for kc in range(NKB):
                for ocn in range(2):
                    pb, pbuf = [(pF, bpF), (pE, bpE)][(kc * 2 + ocn) % 2]
                    def vmm(e, kc=kc, ocn=ocn, pb=pb):
                        for cc in range(8):
                            e.matmul(pb[:, 0:512], regA[:, cc, kc * 128:(kc + 1) * 128],
                                     W2H[:, cc, ocn * 512:(ocn + 1) * 512],
                                     start=(cc == 0), stop=False)
                        return e.matmul(pb[:, 0:512], onesB[0:1, :],
                                        vbrow[0:1, ocn * 512:(ocn + 1) * 512],
                                        start=False, stop=True)
                    emit('tensor', vmm, reads=[bA, bW2, bONEB, bVB], writes=[pbuf])
                    emit('vector', (lambda kc, ocn, pb: lambda e: e.tensor_copy(
                            out=vpx[:, kc, :].rearrange("p (h c) -> p h c", c=65)
                                [:, ocn * 8:(ocn + 1) * 8, 0:64],
                            in_=pb[:, 0:512].rearrange("p (h c) -> p h c", c=64)))(
                                kc, ocn, pb),
                         reads=[pbuf], writes=[bVPX])

            for Hh in range(NHALF):
                ht0 = Hh * HALF

                for e_i in range(NE):
                    for uh in range(2):
                        emit('sync', (lambda e_i, uh: lambda e: e.dma_start(
                                out=W1H[:, :, :],
                                in_=d_w1b[e_i, :, uh * 1024:(uh + 1) * 1024]
                                .rearrange("(c p) h -> p c h", p=128)))(e_i, uh),
                             writes=[bW1], extra=ev_w1p[e_i][uh])
                        emit('sync', (lambda e_i, uh: lambda e: e.dma_start(
                                out=W2H[:, :, :],
                                in_=d_w2b[e_i, uh * 1024:(uh + 1) * 1024, :]
                                .rearrange("(h p) d -> p h d", p=128)))(e_i, uh),
                             writes=[bW2], extra=ev_w2p[e_i][uh])
                        for tsub in range(NCH):
                            t0 = ht0 + tsub * 512
                            for uc in range(8):
                                ph, phb = [(pSC0[:, 0:512], bS0a),
                                           (pSC0[:, 512:1024], bS0b)][uc % 2]
                                emit('tensor', (lambda uc, t0, ph: lambda e: [
                                        e.matmul(ph,
                                                 W1H[:, cc, uc * 128:(uc + 1) * 128],
                                                 xn[:, cc, t0:t0 + 512],
                                                 start=(cc == 0), stop=(cc == 7))
                                        for cc in range(8)][-1])(uc, t0, ph),
                                     reads=[bW1, bXN], writes=[phb])
                                emit('scalar', (lambda uc, uh, e_i, ph, tsub=tsub:
                                        lambda e: e.activation(
                                            out=regA[:, uc,
                                                     tsub * 512:(tsub + 1) * 512],
                                            in_=ph, func=AF.Gelu,
                                            bias=b1T[:, uh * 8 + uc, e_i:e_i + 1])
                                        )(uc, uh, e_i, ph),
                                     reads=[phb, bB1T], writes=[bA])
                            if e_i == NE - 1 and uh == 1 and tsub == NCH - 1:
                                emit('sync', lambda e: e.dma_start(
                                        out=W1H[:, :, :],
                                        in_=d_awb[0].rearrange("(c p) o -> p c o",
                                                               p=128)),
                                     writes=[bW1], extra=ev_aw[0])
                            for tl in range(4):
                                tg = tsub * 4 + tl
                                for ocn in range(2):
                                    py, pyb = [(pC, bpC), (pD, bpD)][ocn]
                                    addb2 = (e_i == NE - 1 and uh == 1)
                                    def ymm(e, tl=tl, ocn=ocn, py=py, e_i=e_i,
                                            uh=uh, tsub=tsub, addb2=addb2):
                                        last = None
                                        for uc in range(8):
                                            c0 = tsub * 512 + tl * 128
                                            last = e.matmul(
                                                py[:, :],
                                                regA[:, uc, c0:c0 + 128],
                                                W2H[:, uc, ocn * 512:(ocn + 1) * 512],
                                                start=(uc == 0),
                                                stop=(uc == 7 and not addb2))
                                        if addb2:
                                            last = e.matmul(
                                                py[:, :], onesB[0:1, :],
                                                rowA[0:1, ocn * 512:(ocn + 1) * 512],
                                                start=False, stop=True)
                                        return last
                                    rd = [bA, bW2] + ([bONEB, bRowA] if addb2 else [])
                                    emit('tensor', ymm, reads=rd, writes=[pyb])
                                    if e_i == 0 and uh == 0:
                                        def yev(e, tg=tg, ocn=ocn, py=py):
                                            return e.tensor_copy(
                                                out=yb[:, tg, ocn * 512:(ocn + 1) * 512],
                                                in_=py[:, :])
                                    else:
                                        def yev(e, tg=tg, ocn=ocn, py=py):
                                            return e.scalar_tensor_tensor(
                                                out=yb[:, tg, ocn * 512:(ocn + 1) * 512],
                                                in0=py[:, :], scalar=1.0,
                                                in1=yb[:, tg, ocn * 512:(ocn + 1) * 512],
                                                op0=ALU.mult, op1=ALU.add)
                                    emit('vector', yev, reads=[pyb, bY], writes=[bY])
                                    if e_i == NE - 1 and uh == 1:
                                        tcol = (ht0 // 128) + tg
                                        emit('scalar', (lambda tg=tg, ocn=ocn,
                                                        tcol=tcol: lambda e:
                                                e.activation(
                                                    out=yb[:, tg,
                                                           ocn * 512:(ocn + 1) * 512],
                                                    in_=yb[:, tg,
                                                           ocn * 512:(ocn + 1) * 512],
                                                    func=AF.Silu,
                                                    scale=ng[:, tcol:tcol + 1]))(),
                                             reads=[bY, bNG], writes=[bY])

                if debug and b == 0 and Hh == 0 and "y" in dbg:
                    emit('gpsimd', lambda e: e.dma_start(
                            out=dbg["y"][:, :],
                            in_=yb.rearrange("p a d -> p (a d)")[:, 0:NTG * D]),
                         reads=[bY], writes=[])

                emit('sync', lambda e: e.dma_start(
                        out=W2H[:, :, :], in_=d_awb[3].rearrange("(c p) o -> p c o", p=128)),
                     writes=[bW2], extra=ev_aw[3])

                for qt in range(NCH):
                    q0 = ht0 + qt * 512
                    for oc in range(8):
                        pb, pbuf = [(pF, bpF), (pE, bpE)][oc % 2]
                        emit('tensor', (lambda oc, q0, pb: lambda e: [
                                e.matmul(pb[:, :],
                                         W1H[:, cc, oc * 128:(oc + 1) * 128],
                                         xn[:, cc, q0:q0 + 512],
                                         start=(cc == 0), stop=(cc == 7))
                                for cc in range(8)][-1])(oc, q0, pb),
                             reads=[bW1, bXN], writes=[pbuf])
                        emit('vector', (lambda oc, pb: lambda e: e.tensor_scalar(
                                out=qt_sb[:, oc, :], in0=pb[:, :],
                                scalar1=qb[:, oc:oc + 1], scalar2=None,
                                op0=ALU.add))(oc, pb),
                             reads=[pbuf, bQB], writes=[bQT[oc]])

                    NKP = NKB // 2
                    for pr2 in range(8):
                        for kp in range(NKP):
                            def scm(e, pr2=pr2, kp=kp):
                                last = None
                                for half in range(2):
                                    kcc = 2 * kp + half
                                    col = slice(half * 512, half * 512 + 512)
                                    e.matmul(pSC0[:, col],
                                             kf[0:64, pr2, kcc * 128:(kcc + 1) * 128],
                                             qt_sb[0:64, pr2, :],
                                             start=True, stop=True)
                                    last = e.matmul(
                                        pSC1[:, col],
                                        kf[64:128, pr2, kcc * 128:(kcc + 1) * 128],
                                        qt_sb[64:128, pr2, :],
                                        start=True, stop=True)
                                return last
                            emit('tensor', scm, reads=[bKF, bQT[pr2]],
                                 writes=[bS0a, bS0b, bS1a, bS1b])
                            emit('scalar', (lambda kp: lambda e: e.activation(
                                    out=Xt[0][:, :], in_=pSC0[:, :], func=AF.Exp,
                                    scale=SCALE_ATT))(kp),
                                 reads=[bS0a, bS0b], writes=[bXt[0]])
                            emit('scalar', (lambda kp: lambda e: e.activation(
                                    out=Xt[1][:, :], in_=pSC1[:, :], func=AF.Exp,
                                    scale=SCALE_ATT))(kp),
                                 reads=[bS1a, bS1b], writes=[bXt[1]])
                            for hp in range(2):
                                h = 2 * pr2 + hp
                                pav, pavb = [(pC, bpC), (pD, bpD)][hp]
                                def avm(e, h=h, kp=kp, pav=pav, hp=hp):
                                    e.matmul(pav[0:65, :],
                                             vpx[:, 2 * kp, h * 65:(h + 1) * 65],
                                             Xt[hp][:, 0:512], start=(kp == 0),
                                             stop=False)
                                    return e.matmul(
                                        pav[0:65, :],
                                        vpx[:, 2 * kp + 1, h * 65:(h + 1) * 65],
                                        Xt[hp][:, 512:1024],
                                        start=False, stop=(kp == NKP - 1))
                                emit('tensor', avm, reads=[bVPX, bXt[hp]],
                                     writes=[pavb])
                        for hp, (pav, pavb) in enumerate([(pC, bpC), (pD, bpD)]):
                            emit('scalar', (lambda pav: lambda e: e.activation(
                                    out=rtmp[:, :], in_=pav[64:65, :],
                                    func=AF.Ln))(pav),
                                 reads=[pavb], writes=[bRT])
                            emit('scalar', (lambda hp: lambda e: e.activation(
                                    out=recips[0:1, hp * 512:(hp + 1) * 512],
                                    in_=rtmp[:, :], func=AF.Exp, scale=-1.0))(hp),
                                 reads=[bRT], writes=[bREC])
                        def rbm(e):
                            e.matmul(pE[:, :], selA[0:1, :], recips[0:1, 0:512],
                                     start=True, stop=False)
                            return e.matmul(pE[:, :], selB[0:1, :],
                                            recips[0:1, 512:1024],
                                            start=False, stop=True)
                        emit('tensor', rbm, reads=[bSelA, bSelB, bREC], writes=[bpE])
                        rbs = pr2 % 2
                        emit('vector', (lambda rbs: lambda e: e.tensor_copy(
                                out=RB[rbs][:, :], in_=pE[:, :]))(rbs),
                             reads=[bpE], writes=[bRB[rbs]])
                        emit('vector', (lambda pr2, rbs: lambda e: e.tensor_tensor(
                                out=attn_o[0:64, pr2, :], in0=pC[0:64, :],
                                in1=RB[rbs][0:64, :], op=ALU.mult))(pr2, rbs),
                             reads=[bpC, bRB[rbs]], writes=[bAO])
                        emit('vector', (lambda pr2, rbs: lambda e: e.tensor_tensor(
                                out=attn_o[64:128, pr2, :], in0=pD[0:64, :],
                                in1=RB[rbs][64:128, :], op=ALU.mult))(pr2, rbs),
                             reads=[bpD, bRB[rbs]], writes=[bAO])

                    for qq in range(4):
                        tg = qt * 4 + qq
                        for ocn in range(2):
                            pb, pbuf = [(pF, bpF), (pE, bpE)][ocn]
                            def smm(e, qq=qq, ocn=ocn, pb=pb):
                                for cc in range(8):
                                    e.matmul(pb[:, :], attn_o[:, cc, qq * 128:(qq + 1) * 128],
                                             W2H[:, cc, ocn * 512:(ocn + 1) * 512],
                                             start=(cc == 0), stop=False)
                                return e.matmul(pb[:, :], onesB[0:1, :],
                                                obrow[0:1, ocn * 512:(ocn + 1) * 512],
                                                start=False, stop=True)
                            emit('tensor', smm, reads=[bAO, bW2, bONEB, bOB],
                                 writes=[pbuf])
                            sgs = ocn
                            emit('vector', (lambda tg, ocn, pb, sgs: lambda e:
                                    e.tensor_tensor(
                                        out=sg_sl[sgs][:, :], in0=pb[:, :],
                                        in1=yb[:, tg, ocn * 512:(ocn + 1) * 512],
                                        op=ALU.mult))(tg, ocn, pb, sgs),
                                 reads=[pbuf, bY], writes=[bSG[sgs]])
                            r0 = brow + ht0 + tg * 128
                            emit('gpsimd', (lambda r0, ocn, sgs: lambda e: e.dma_start(
                                    out=d_out[r0:r0 + 128, ocn * 512:(ocn + 1) * 512],
                                    in_=sg_sl[sgs][:, :], accum_op=ALU.add))(r0, ocn, sgs),
                                 reads=[bSG[sgs]], extra=[ev_xcopy])

            if debug and b == 0:
                if "xn" in dbg:
                    emit('gpsimd', lambda e: e.dma_start(
                            out=dbg["xn"][:, :],
                            in_=xn.rearrange("p a s -> p (a s)")[:, :]),
                         reads=[bXN])
                if "ng" in dbg:
                    emit('gpsimd', lambda e: e.dma_start(out=dbg["ng"][:, :], in_=ng[:, :]),
                         reads=[bNG])
                if "kf" in dbg:
                    emit('gpsimd', lambda e: e.dma_start(
                            out=dbg["kf"][:, :],
                            in_=kf.rearrange("p a s -> p (a s)")[:, :]),
                         reads=[bKF])
                if "vpx" in dbg:
                    emit('gpsimd', lambda e: e.dma_start(
                            out=dbg["vpx"][:, :],
                            in_=vpx.rearrange("p a s -> p (a s)")[:, :]),
                         reads=[bVPX])

        fin = [(s, cnt[id(s)]) for s in LS + LG + [PE, SC, VE] if cnt[id(s)] > 0]
        items.append(('sync', fin, lambda e: None, None, 0))

        waited = {}
        for (_, ws, _, _, _) in items:
            for (s, v) in ws:
                waited.setdefault(id(s), set()).add(v)
        unit = {}
        for s in LS + LG:
            unit[id(s)] = 16
        for s in (PE, SC, VE):
            unit[id(s)] = 1
        rank = {sid: {v: (i + 1) for i, v in enumerate(sorted(vals))}
                for sid, vals in waited.items()}
        hw = {}

        def body(eng, name):
            for (e, ws, fn, sem, evv) in items:
                if e != name:
                    continue
                for (s, v) in ws:
                    nv = rank[id(s)][v] * unit[id(s)]
                    key = (name, id(s))
                    if hw.get(key, -1) >= nv:
                        continue
                    hw[key] = nv
                    eng.wait_ge(s, nv)
                ins = fn(eng)
                if ins is not None and sem is not None \
                        and evv in waited.get(id(sem), ()):
                    ins.then_inc(sem, unit[id(sem)])

        block.sync(lambda e: body(e, 'sync'))
        block.gpsimd(lambda e: body(e, 'gpsimd'))
        block.tensor(lambda e: body(e, 'tensor'))
        block.scalar(lambda e: body(e, 'scalar'))
        block.vector(lambda e: body(e, 'vector'))

    return nc



def host_layout(inputs):
    f = np.ascontiguousarray
    a = lambda k: np.asarray(inputs[k], np.float32)
    common = dict(
        qwT=f(a("L0_attn_q_w").T), kwT=f(a("L0_attn_k_w").T),
        vwT=f(a("L0_attn_v_w").T), owT=f(a("L0_attn_out_w").T),
        w1=f(a("L0_ffn_w1")), w2=f(a("L0_ffn_w2")),
        gwT=f(a("L0_ffn_gate_w").T),
        b1T=f(a("L0_ffn_b1").T.reshape(16, 128, NE).transpose(1, 0, 2)
              .reshape(128, 16 * NE)),
        b2=f(a("L0_ffn_b2")),
        qb=f(a("L0_attn_q_b").reshape(8, 128).T),
        kb=f(a("L0_attn_k_b").reshape(8, 128).T),
        vb=f(a("L0_attn_v_b")[None, :]), ob=f(a("L0_attn_out_b")[None, :]),
        gb=f(a("L0_ffn_gate_b")[None, :]), rw=f(a("L0_ffn_rms_w")[None, :]),
        cw=f(a("L0_attn_conv_w")[:, 0, :].reshape(8, 128, 3).transpose(1, 0, 2)
             .reshape(128, 24)),
        cb=f(a("L0_attn_conv_b").reshape(8, 128).T),
        ident=np.eye(128, dtype=np.float32),
    )
    return common


def kernel(**inputs):
    x = np.asarray(inputs["L0_attn_x"], np.float32)
    bpc = x.shape[0] // NCORES
    seq = x.shape[1]
    common = host_layout(inputs)
    nc = build_program(seq=seq, bpc=bpc, debug=False)
    in_maps = []
    for c in range(NCORES):
        m = dict(common)
        m["x"] = np.ascontiguousarray(
            x[c * bpc:(c + 1) * bpc].reshape(bpc * seq, D))
        in_maps.append(m)
    res = run_bass_kernel_spmd(nc, in_maps, core_ids=list(range(NCORES)))
    return np.concatenate(
        [r["out"].reshape(bpc, seq, D) for r in res.results], axis=0)
